# version 15
# baseline (speedup 1.0000x reference)
import numpy as np
import ml_dtypes
from contextlib import ExitStack

import concourse.bass as bass
import concourse.mybir as mybir
from concourse.bass_utils import run_bass_kernel_spmd

F32 = mybir.dt.float32
BF16 = mybir.dt.float16
I32 = mybir.dt.int32
AF = mybir.ActivationFunctionType
ALU = mybir.AluOpType
AX = mybir.AxisListType.X

T = 2048
D = 4096
HKV = 8
HQ = 32
NT = T // 128
NI = D // 128
OCW = 256
NOC = D // OCW
NK = NOC * NT
OC6 = 512
NOC6 = D // OC6
NK6 = NOC6 * NT
NQ8 = 6
DEBUG_P2 = False
DEBUG_HW = False
LN_EPS = 1e-5
ATT_EPS = 1e-10


class Sem:
    def __init__(self, nc, es, name):
        self.h = es.enter_context(nc.semaphore(name))
        self.cnt = 0

    def inc(self, instr, k=1):
        instr.then_inc(self.h, k)
        self.cnt += k
        return instr


def build_program():
    nc = bass.Bass("TRN2", target_bir_lowering=False, detect_race_conditions=False)

    x_d = nc.dram_tensor("x", [T, D], F32, kind="ExternalInput").ap()
    tq_d = nc.dram_tensor("tq", [HQ, 128, NI, 128], F32, kind="ExternalInput").ap()
    tk_d = nc.dram_tensor("tk", [HKV, 128, NI, 128], F32, kind="ExternalInput").ap()
    tv_d = nc.dram_tensor("tv", [HKV, 128, NI, 128], F32, kind="ExternalInput").ap()
    to_d = nc.dram_tensor("to", [NOC6, 2, 128, HQ // 2, OC6], F32, kind="ExternalInput").ap()
    pk_d = nc.dram_tensor("pk", [D, 512], I32, kind="ExternalInput").ap()
    sc_d = nc.dram_tensor("sc", [128, NI * 32], F32, kind="ExternalInput").ap()
    bq_d = nc.dram_tensor("bq", [128, HQ], F32, kind="ExternalInput").ap()
    bk_d = nc.dram_tensor("bk", [128, HKV], F32, kind="ExternalInput").ap()
    bv_d = nc.dram_tensor("bv", [128, HKV], F32, kind="ExternalInput").ap()
    bor_d = nc.dram_tensor("bor", [1, D], F32, kind="ExternalInput").ap()
    fbr_d = nc.dram_tensor("fbr", [1, D], F32, kind="ExternalInput").ap()
    out_d = nc.dram_tensor("out", [T, D], F32, kind="ExternalOutput").ap()

    att_s = nc.dram_tensor("att_s", [HQ, 128, T], BF16, kind="Internal").ap()
    if DEBUG_HW:
        dbg_xnt = nc.dram_tensor("dbg_xnt", [128, NI * T], BF16, kind="ExternalOutput").ap()
        dbg_att = nc.dram_tensor("dbg_att", [HQ, 128, T], BF16, kind="ExternalOutput").ap()
        dbg_g = nc.dram_tensor("dbg_g", [NOC, NT, 128, OCW], BF16, kind="ExternalOutput").ap()
        dbg_kvt = nc.dram_tensor("dbg_kvt", [128, HKV * 128], BF16, kind="ExternalOutput").ap()
        dbg_nrm = nc.dram_tensor("dbg_nrm", [128, HKV], F32, kind="ExternalOutput").ap()
        dbg_rcp = nc.dram_tensor("dbg_rcp", [128, HKV], F32, kind="ExternalOutput").ap()
        dbg_big = nc.dram_tensor("dbg_big", [128, NI * T], BF16, kind="ExternalOutput").ap()
    if DEBUG_P2:
        dbg_kn = nc.dram_tensor("dbg_kn", [128, T], BF16, kind="Internal").ap()
        dbg_vn = nc.dram_tensor("dbg_vn", [128, T], BF16, kind="Internal").ap()
        dbg_kt = nc.dram_tensor("dbg_kt", [128, T], BF16, kind="Internal").ap()
        dbg_vt = nc.dram_tensor("dbg_vt", [128, T], BF16, kind="Internal").ap()
    g_s = nc.dram_tensor("g_s", [NOC, NT, 128, OCW], BF16, kind="Internal").ap()

    ident_d = nc.inline_tensor(np.eye(128, dtype=np.float16), "ident_c").ap()
    ones_d = nc.inline_tensor(np.ones((1, 128), dtype=np.float16), "ones_c").ap()

    es = ExitStack()
    SB = lambda n, sh, d: es.enter_context(nc.sbuf_tensor(n, sh, d))

    big = SB("big", [128, NI * T], BF16)
    norms = SB("norms", [128, HKV], F32)
    recip = SB("recip", [128, HKV], F32)
    bq_sb = SB("bq_sb", [128, HQ], F32)
    bk_sb = SB("bk_sb", [128, HKV], F32)
    bv_sb = SB("bv_sb", [128, HKV], F32)
    ident = SB("ident", [128, 128], BF16)
    ones_r = SB("ones_r", [1, 128], BF16)
    stat = SB("stat", [128, 64], F32)
    junkA = SB("junkA", [128, 128], BF16)
    junkB = SB("junkB", [128, 128], BF16)
    mid = ExitStack()
    wkb = [mid.enter_context(nc.sbuf_tensor(f"wkb{i}", [128, NI * 128], BF16))
           for i in range(2)]
    sc_sb = mid.enter_context(nc.sbuf_tensor("sc_sb", [128, NI * 32], F32))
    kvt_sb = mid.enter_context(nc.sbuf_tensor("kvt_sb", [128, HKV * 128], BF16))

    bigv = big.rearrange("p (a t) -> p a t", a=NI)

    names = ("bar", "cst", "stats", "var", "rstd", "xn", "tr", "trev",
             "mmk", "mmv", "t1k", "ek", "phik", "cpv",
             "trk", "trkev", "trv", "trvev", "kvt", "kvtev", "nrm", "rcp",
             "deq", "trw", "trwev", "gmm", "gev",
             "qmm", "t1q", "eq", "phiq", "amm", "aev",
             "p5g", "p5s", "p5a", "smm", "oev", "m1", "m2", "fbmm", "fbev", "gvb")
    sems = {n: Sem(nc, es, "s_" + n) for n in names}
    s = type("S", (), sems)
    P = lambda n: [Sem(nc, es, f"s_{n}{i}") for i in range(2)]
    skk, skv, spk, sgst = P("kk"), P("kv"), P("pk"), P("gst")
    sxb = [Sem(nc, es, f"s_xb{i}") for i in range(4)]
    swq, saw, swo, sfb, sbb = P("wq"), P("aw"), P("wo"), P("fb"), P("bb")
    swo3 = [Sem(nc, es, f"s_w3{i}") for i in range(4)]
    sgi, sxi, sos = P("gi"), P("xi"), P("os")
    sdbgx = Sem(nc, es, "s_dbgx")

    NENG = 5
    bar_n = [0]

    def barrier():
        bar_n[0] += 1
        tgt = NENG * bar_n[0]
        with nc.Block() as blk:
            for name in ("sync", "gpsimd", "tensor", "vector", "scalar"):
                def body(e, tgt=tgt):
                    s.bar.cnt += 1
                    e.nop().then_inc(s.bar.h, 1)
                    e.wait_ge(s.bar.h, tgt)
                getattr(blk, name)(body)

    with nc.Block() as blk:
        @blk.vector
        def _(v):
            v.memset(junkA[:], 0.0)

        @blk.gpsimd
        def _(g):
            s.cst.inc(g.dma_start(out=ident[:], in_=ident_d), 16)
            s.cst.inc(g.dma_start(out=ones_r[:], in_=ones_d), 16)
            s.cst.inc(g.dma_start(out=bq_sb[:], in_=bq_d), 16)
            s.cst.inc(g.dma_start(out=bk_sb[:], in_=bk_d), 16)
            s.cst.inc(g.dma_start(out=bv_sb[:], in_=bv_d), 16)
            s.cst.inc(g.dma_start(out=sc_sb[:], in_=sc_d), 16)

    with ExitStack() as ar, nc.named_scope("PH1"):
        A = lambda n, sh, d: ar.enter_context(nc.sbuf_tensor(n, sh, d))
        xb = [A(f"p1x{i}", [128, D], BF16) for i in range(4)]
        xnb = [A(f"p1xn{i}", [128, D], BF16) for i in range(2)]
        with nc.psum_tensor("p1ps", [128, 4096], BF16) as ps, nc.Block() as blk:

            @blk.gpsimd
            def _(g):
                for tt in range(NT):
                    if tt >= 4:
                        g.wait_ge(s.stats.h, tt - 3)
                        g.wait_ge(s.xn.h, tt - 3)
                    sxb[tt % 4].inc(g.dma_start(out=xb[tt % 4][:],
                                                in_=x_d[tt * 128:(tt + 1) * 128, :]), 16)
                    if tt == 2:
                        skk[0].inc(g.dma_start(
                            out=wkb[0][:], in_=tk_d[0].rearrange("p i o -> p (i o)")), 16)

            @blk.scalar
            def _(sc_e):
                def rstd_step(pt):
                    psl = (pt % 4) * 16
                    sc_e.wait_ge(s.var.h, pt + 1)
                    s.rstd.inc(sc_e.activation(stat[:, psl + 8:psl + 9],
                                               stat[:, psl + 7:psl + 8], AF.Sqrt), 1)

                for tt in range(NT):
                    if tt >= 1:
                        rstd_step(tt - 1)
                    sc_e.wait_ge(sxb[tt % 4].h, 16 * (tt // 4 + 1))
                    if tt >= 4:
                        sc_e.wait_ge(s.xn.h, tt - 3)
                    sl = (tt % 4) * 16
                    trash = bigv[:, 0:NI, tt * 128:(tt + 1) * 128]
                    sc_e.activation(trash, xb[tt % 4][:], AF.Square,
                                    accum_out=stat[:, sl + 1:sl + 2])
                    s.stats.inc(sc_e.activation(trash, xb[tt % 4][:], AF.Copy,
                                                accum_out=stat[:, sl:sl + 1]), 1)
                rstd_step(NT - 1)

            @blk.vector
            def _(v):
                def spacer():
                    v.tensor_copy(junkB[:], junkA[:])
                    v.tensor_copy(junkA[:], junkB[:])

                def seg_a(tt):
                    v.wait_ge(s.stats.h, tt + 1)
                    sl = (tt % 4) * 16
                    v.tensor_scalar(stat[:, sl + 2:sl + 3], stat[:, sl:sl + 1],
                                    1.0 / D, None, op0=ALU.mult)
                    v.tensor_scalar(stat[:, sl + 3:sl + 4], stat[:, sl:sl + 1],
                                    -1.0 / D, None, op0=ALU.mult)
                    v.tensor_scalar(stat[:, sl + 4:sl + 5], stat[:, sl + 1:sl + 2],
                                    1.0 / D, LN_EPS,
                                    op0=ALU.mult, op1=ALU.add)
                    spacer()
                    v.tensor_scalar(stat[:, sl + 5:sl + 6], stat[:, sl + 2:sl + 3],
                                    stat[:, sl + 2:sl + 3], None,
                                    op0=ALU.mult)
                    spacer()
                    v.tensor_scalar(stat[:, sl + 6:sl + 7], stat[:, sl + 4:sl + 5],
                                    stat[:, sl + 5:sl + 6], None,
                                    op0=ALU.subtract)
                    spacer()
                    s.var.inc(v.reciprocal(stat[:, sl + 7:sl + 8],
                                           stat[:, sl + 6:sl + 7]), 1)

                def seg_c(pt):
                    v.wait_ge(s.rstd.h, pt + 1)
                    sl = (pt % 4) * 16
                    v.tensor_scalar(stat[:, sl + 9:sl + 10], stat[:, sl + 3:sl + 4],
                                    stat[:, sl + 8:sl + 9], None, op0=ALU.mult)
                    spacer()
                    if pt >= 2:
                        v.wait_ge(s.tr.h, 4 * (pt - 1))
                    s.xn.inc(v.tensor_scalar(xnb[pt % 2][:], xb[pt % 4][:],
                                             stat[:, sl + 8:sl + 9], stat[:, sl + 9:sl + 10],
                                             op0=ALU.mult, op1=ALU.add), 1)

                def evacs(pt):
                    for b in range(4):
                        v.wait_ge(s.tr.h, 4 * pt + b + 1)
                        s.trev.inc(v.tensor_copy(
                            bigv[:, b * 8:(b + 1) * 8, pt * 128:(pt + 1) * 128],
                            ps[:, b * 1024:(b + 1) * 1024]), 1)

                for t in range(NT):
                    if t >= 2:
                        evacs(t - 2)
                    if t >= 1:
                        seg_c(t - 1)
                    seg_a(t)
                seg_c(NT - 1)
                evacs(NT - 2)
                evacs(NT - 1)

            @blk.tensor
            def _(te):
                te.wait_ge(s.cst.h, 96)
                for tt in range(NT):
                    te.wait_ge(s.xn.h, tt + 1)
                    if tt >= 1:
                        te.wait_ge(s.trev.h, 4 * tt)
                    for j in range(NI):
                        b, jj = j // 8, j % 8
                        ins = te.transpose(
                            ps[:, b * 1024 + jj * 128: b * 1024 + (jj + 1) * 128],
                            xnb[tt % 2][:, j * 128:(j + 1) * 128], ident[:])
                        if jj == 7:
                            s.tr.inc(ins, 1)

    barrier()

    with ExitStack() as ar, nc.named_scope("PH2"):
        A = lambda n, sh, d: ar.enter_context(nc.sbuf_tensor(n, sh, d))
        wvb = [A(f"p2wv{i}", [128, NI * 128], BF16) for i in range(2)]
        t1b = A("p2t1", [128, T], BF16)
        t2b = A("p2t2", [128, T], BF16)
        t3b = A("p2t3", [128, T], BF16)
        kTb = [A(f"p2kT{i}", [128, T], BF16) for i in range(2)]
        vTb = [A(f"p2vT{i}", [128, T], BF16) for i in range(2)]
        k_nat = A("p2kn", [128, T], BF16)
        v_nat = A("p2vn", [128, T], BF16)
        with nc.psum_tensor("p2pk", [128, 2048], F32) as pk_ps, \
             nc.psum_tensor("p2pv", [128, 2048], F32) as pv_ps, nc.Block() as blk:
            pk_bf = pk_ps[:].bitcast(BF16)

            @blk.gpsimd
            def _(g):
                skv[0].inc(g.dma_start(out=wvb[0][:],
                                       in_=tv_d[0].rearrange("p i o -> p (i o)")), 16)
                for h in range(1, HKV):
                    if h >= 2:
                        g.wait_ge(s.mmk.h, h - 1)
                    skk[h % 2].inc(g.dma_start(out=wkb[h % 2][:],
                                               in_=tk_d[h].rearrange("p i o -> p (i o)")), 16)
                    if h >= 2:
                        g.wait_ge(s.mmv.h, h - 1)
                    skv[h % 2].inc(g.dma_start(out=wvb[h % 2][:],
                                               in_=tv_d[h].rearrange("p i o -> p (i o)")), 16)
                if DEBUG_P2:
                    g.wait_ge(s.kvtev.h, HKV)
                    g.dma_start(out=dbg_kn, in_=k_nat[:])
                    g.dma_start(out=dbg_vn, in_=v_nat[:])
                    g.dma_start(out=dbg_kt, in_=kTb[1][:])
                    g.dma_start(out=dbg_vt, in_=vTb[1][:])

            @blk.tensor
            def _(te):
                def kvt_mms(ph):
                    te.wait_ge(s.trkev.h, 2 * ph + 2)
                    te.wait_ge(s.trvev.h, 2 * ph + 2)
                    for tt in range(NT):
                        ins = te.matmul(pv_ps[:, 0:128],
                                        k_nat[:, tt * 128:(tt + 1) * 128],
                                        v_nat[:, tt * 128:(tt + 1) * 128],
                                        start=(tt == 0), stop=(tt == NT - 1))
                    s.kvt.inc(ins, 1)
                for h in range(HKV):
                    te.wait_ge(skk[h % 2].h, 16 * (h // 2 + 1))
                    for c in range(4):
                        if h >= 1:
                            if c < 2:
                                te.wait_ge(s.trkev.h, 2 * (h - 1) + c + 1)
                            else:
                                te.wait_ge(s.trvev.h, 2 * (h - 1) + c - 1)
                        if c == 3 and h >= 1:
                            kvt_mms(h - 1)
                        for i in range(NI):
                            ins = te.matmul(pk_ps[:, c * 512:(c + 1) * 512],
                                            wkb[h % 2][:, i * 128:(i + 1) * 128],
                                            bigv[:, i, c * 512:(c + 1) * 512],
                                            start=(i == 0), stop=(i == NI - 1))
                    s.mmk.inc(ins, 1)
                    if h == HKV - 1:
                        pass
                    te.wait_ge(skv[h % 2].h, 16 * (h // 2 + 1))
                    if h >= 1:
                        te.wait_ge(s.kvtev.h, h)
                    for i in range(NI):
                        for c in range(4):
                            ins = te.matmul(pv_ps[:, c * 512:(c + 1) * 512],
                                            wvb[h % 2][:, i * 128:(i + 1) * 128],
                                            bigv[:, i, c * 512:(c + 1) * 512],
                                            start=(i == 0), stop=(i == NI - 1))
                    s.mmv.inc(ins, 1)
                    te.wait_ge(s.phik.h, h + 1)
                    for tt in range(NT):
                        b, jj = tt // 8, tt % 8
                        ins = te.transpose(
                            pk_bf[:, b * 1024 + jj * 128: b * 1024 + (jj + 1) * 128],
                            kTb[h % 2][:, tt * 128:(tt + 1) * 128], ident[:])
                        if jj == 7:
                            s.trk.inc(ins, 1)
                    te.wait_ge(s.cpv.h, h + 1)
                    for tt in range(NT):
                        b, jj = 2 + tt // 8, tt % 8
                        ins = te.transpose(
                            pk_bf[:, b * 1024 + jj * 128: b * 1024 + (jj + 1) * 128],
                            vTb[h % 2][:, tt * 128:(tt + 1) * 128], ident[:])
                        if jj == 7:
                            s.trv.inc(ins, 1)
                    if h == HKV - 1:
                        kvt_mms(h)

            @blk.vector
            def _(v):
                for h in range(HKV):
                    v.wait_ge(s.mmk.h, h + 1)
                    if h >= 1:
                        v.wait_ge(s.ek.h, h)
                    if h >= 2:
                        v.wait_ge(s.trk.h, 2 * (h - 1))
                    s.t1k.inc(v.tensor_scalar(t1b[:], pk_ps[:], bk_sb[:, h:h + 1], 0.0,
                                              op0=ALU.add, op1=ALU.min), 1)
                    v.tensor_scalar(t3b[:], pk_ps[:], bk_sb[:, h:h + 1], 0.0,
                                    op0=ALU.add, op1=ALU.max)
                    v.wait_ge(s.ek.h, h + 1)
                    s.phik.inc(v.tensor_tensor(kTb[h % 2][:], t2b[:], t3b[:],
                                               op=ALU.add), 1)
                    v.wait_ge(s.mmv.h, h + 1)
                    if h >= 2:
                        v.wait_ge(s.trv.h, 2 * (h - 1))
                    s.cpv.inc(v.tensor_scalar(vTb[h % 2][:], pv_ps[:],
                                              bv_sb[:, h:h + 1], None,
                                              op0=ALU.add), 1)
                    s.nrm.inc(v.reduce_sum(norms[:, h:h + 1], kTb[h % 2][:], axis=AX), 1)
                    for b in range(2):
                        v.wait_ge(s.trk.h, 2 * h + b + 1)
                        if h >= 1 and b == 0:
                            v.wait_ge(s.kvt.h, h)
                        s.trkev.inc(v.tensor_copy(k_nat[:, b * 1024:(b + 1) * 1024],
                                                  pk_bf[:, b * 1024:(b + 1) * 1024]), 1)
                    for b in range(2):
                        v.wait_ge(s.trv.h, 2 * h + b + 1)
                        if h >= 1 and b == 0:
                            v.wait_ge(s.kvt.h, h)
                        s.trvev.inc(v.tensor_copy(v_nat[:, b * 1024:(b + 1) * 1024],
                                                  pk_bf[:, (2 + b) * 1024:(3 + b) * 1024]), 1)
                    v.wait_ge(s.kvt.h, h + 1)
                    s.kvtev.inc(v.tensor_copy(kvt_sb[:, h * 128:(h + 1) * 128],
                                              pv_ps[:, 0:128]), 1)
                v.wait_ge(s.nrm.h, HKV)
                v.tensor_scalar(stat[:, 0:HKV], norms[:], ATT_EPS, None, op0=ALU.add)
                v.tensor_copy(junkB[:], junkA[:])
                v.tensor_copy(junkA[:], junkB[:])
                s.rcp.inc(v.reciprocal(recip[:], stat[:, 0:HKV]), 1)

            @blk.scalar
            def _(sc_e):
                for h in range(HKV):
                    sc_e.wait_ge(s.t1k.h, h + 1)
                    if h >= 1:
                        sc_e.wait_ge(s.phik.h, h)
                    s.ek.inc(sc_e.activation(t2b[:], t1b[:], AF.Exp), 1)

    barrier()

    with ExitStack() as ar, nc.named_scope("PH4"):
        A = lambda n, sh, d: ar.enter_context(nc.sbuf_tensor(n, sh, d))
        WT = [A(f"p4wt{i}", [128, NI * OCW], BF16) for i in range(2)]
        pkb = [A(f"p4pk{i}", [128, 512], I32) for i in range(2)]
        wi32 = A("p4wi", [128, 1024], I32)
        wq8 = [A(f"p4w8{i}", [128, 1024], BF16) for i in range(NQ8)]
        gst = [A(f"p4gs{i}", [128, OCW], BF16) for i in range(2)]
        fb_oc = [A(f"p4fb{i}", [1, OCW], BF16) for i in range(2)]
        fbb = [A(f"p4fbb{i}", [128, OCW], BF16) for i in range(2)]
        gtmp = [A(f"p4gt{i}", [128, OCW], BF16) for i in range(2)]
        with nc.psum_tensor("p4ga", [128, 512], F32) as pgA, \
             nc.psum_tensor("p4gb", [128, 512], F32) as pgB, \
             nc.psum_tensor("p4ta", [128, 1024], BF16) as ptA, \
             nc.psum_tensor("p4tb", [128, 1024], BF16) as ptB, \
             nc.psum_tensor("p4bc", [128, 512], F32) as pbc, nc.Block() as blk:
            pg = [pgA, pgB]
            ptr = [ptA, ptB]

            @blk.gpsimd
            def _(g):
                if DEBUG_HW:
                    sdbgx.inc(g.dma_start(out=dbg_xnt, in_=big[:]), 16)
                swq[0].inc(g.dma_start(out=wkb[0][:],
                                       in_=tq_d[0].rearrange("p i o -> p (i o)")), 16)
                swq[1].inc(g.dma_start(out=wkb[1][:],
                                       in_=tq_d[1].rearrange("p i o -> p (i o)")), 16)
                for oc in range(NOC):
                    if oc >= 2:
                        g.wait_ge(s.gmm.h, NT * (oc - 1))
                    sfb[oc % 2].inc(g.dma_start(out=fb_oc[oc % 2][:],
                                                in_=fbr_d[0:1, oc * OCW:(oc + 1) * OCW]), 16)

            @blk.sync
            def _(sy):
                def packed(ptg):
                    if ptg >= 2:
                        sy.wait_ge(s.deq.h, 4 * (ptg - 1))
                    spk[ptg % 2].inc(sy.dma_start(out=pkb[ptg % 2][:],
                                                  in_=pk_d[ptg * 128:(ptg + 1) * 128, :]), 16)
                for ptg in range(4):
                    packed(ptg)
                for oc in range(NOC):
                    for ptg in (2 * oc + 4, 2 * oc + 5):
                        if ptg < NI:
                            packed(ptg)
                    for tt in range(NT):
                        k = oc * NT + tt
                        sy.wait_ge(s.gev.h, k + 1)
                        sgst[k % 2].inc(sy.dma_start(out=g_s[oc, tt],
                                                     in_=gst[k % 2][:]), 16)

            @blk.vector
            def _(v):
                kq_ctr = [0]

                def gvb_step(v, kq):
                    oc_b = kq // NT
                    if kq % NT == 0:
                        v.wait_ge(s.fbmm.h, oc_b + 1)
                        if oc_b >= 2:
                            v.wait_ge(s.gvb.h, NT * (oc_b - 1))
                        s.fbev.inc(v.tensor_copy(fbb[oc_b % 2][:],
                                                 pbc[:, 0:OCW]), 1)
                    v.wait_ge(s.gmm.h, kq + 1)
                    if kq >= 2:
                        v.wait_ge(s.gev.h, kq - 1)
                    s.gvb.inc(v.tensor_tensor(
                        gtmp[kq % 2][:], pg[kq % 2][:, 0:OCW],
                        fbb[oc_b % 2][:], op=ALU.add), 1)

                def evac(j):
                    ptg2, iq2 = divmod(j, 4)
                    oc2, ptl2 = divmod(ptg2, 2)
                    if j % 8 == 0 and oc2 >= 2:
                        v.wait_ge(s.gmm.h, NT * (oc2 - 1))
                    dst = WT[oc2 % 2].rearrange("p (i o) -> p i o", i=NI)[
                        :, iq2 * 8:(iq2 + 1) * 8, ptl2 * 128:(ptl2 + 1) * 128]
                    s.trwev.inc(v.tensor_copy(dst, ptr[j % 2][:]), 1)
                for ptg in range(NI):
                    v.wait_ge(spk[ptg % 2].h, 16 * (ptg // 2 + 1))
                    for iq in range(4):
                        qk = 4 * ptg + iq
                        if qk >= NQ8:
                            v.wait_ge(s.trw.h, qk - NQ8 + 1)
                        for e in range(8):
                            v.tensor_scalar(wi32[:, e::8],
                                            pkb[ptg % 2][:, iq * 128:(iq + 1) * 128],
                                            4 * e, 28,
                                            op0=ALU.arith_shift_left,
                                            op1=ALU.arith_shift_right)
                        for gl in range(8):
                            gg = iq * 8 + gl
                            ins = v.tensor_scalar(wq8[qk % NQ8][:, gl * 128:(gl + 1) * 128],
                                                  wi32[:, gl * 128:(gl + 1) * 128],
                                                  sc_sb[:, ptg * 32 + gg: ptg * 32 + gg + 1],
                                                  None, op0=ALU.mult)
                        s.deq.inc(ins, 1)
                        if qk >= NQ8:
                            evac(qk - NQ8)
                        if qk >= 14:
                            for _ in range(2):
                                kq = kq_ctr[0]
                                if kq < NK:
                                    gvb_step(v, kq)
                                    kq_ctr[0] += 1
                for j in range(4 * NI - NQ8, 4 * NI):
                    target = min(NK, 16 * (j // 8 - 1) + 2 * (j % 8) + 4,
                                 16 * (j // 8))
                    while kq_ctr[0] < target:
                        gvb_step(v, kq_ctr[0])
                        kq_ctr[0] += 1
                    v.wait_ge(s.trw.h, j + 1)
                    evac(j)
                while kq_ctr[0] < NK:
                    gvb_step(v, kq_ctr[0])
                    kq_ctr[0] += 1

            @blk.tensor
            def _(te):
                def transposes(ptg):
                    for iq in range(4):
                        qk = 4 * ptg + iq
                        te.wait_ge(s.deq.h, qk + 1)
                        if qk >= 2:
                            te.wait_ge(s.trwev.h, qk - 1)
                        for il in range(8):
                            ins = te.transpose(ptr[qk % 2][:, il * 128:(il + 1) * 128],
                                               wq8[qk % NQ8][:, il * 128:(il + 1) * 128],
                                               ident[:])
                        s.trw.inc(ins, 1)
                def tr_quarter(qk):
                    te.wait_ge(s.deq.h, qk + 1)
                    if qk >= 2:
                        te.wait_ge(s.trwev.h, qk - 1)
                    for il in range(8):
                        ins = te.transpose(ptr[qk % 2][:, il * 128:(il + 1) * 128],
                                           wq8[qk % NQ8][:, il * 128:(il + 1) * 128],
                                           ident[:])
                    s.trw.inc(ins, 1)
                for qk in range(8):
                    tr_quarter(qk)
                for oc in range(NOC):
                    te.wait_ge(s.trwev.h, 8 * (oc + 1))
                    te.wait_ge(sfb[oc % 2].h, 16 * (oc // 2 + 1))
                    if oc >= 1:
                        te.wait_ge(s.fbev.h, oc)
                    s.fbmm.inc(te.matmul(pbc[:, 0:OCW], ones_r[0:1, :],
                                         fb_oc[oc % 2][0:1, :],
                                         start=True, stop=True), 1)
                    for tt in range(NT):
                        k = oc * NT + tt
                        if tt % 2 == 0:
                            qk = 8 * (oc + 1) + tt // 2
                            if qk < 4 * NI:
                                tr_quarter(qk)
                        if k >= 2:
                            te.wait_ge(s.gev.h, k - 1)
                        pgk = pg[k % 2]
                        for i in range(NI):
                            ins = te.matmul(pgk[:, 0:OCW],
                                            bigv[:, i, tt * 128:(tt + 1) * 128],
                                            WT[oc % 2][:, i * OCW:(i + 1) * OCW],
                                            start=(i == 0), stop=(i == NI - 1))
                        s.gmm.inc(ins, 1)

            @blk.scalar
            def _(sc_e):
                for k in range(NK):
                    sc_e.wait_ge(s.gvb.h, k + 1)
                    if k >= 2:
                        sc_e.wait_ge(sgst[k % 2].h, 16 * (k // 2))
                    s.gev.inc(sc_e.activation(gst[k % 2][:], gtmp[k % 2][:],
                                              AF.Silu), 1)

    barrier()

    with ExitStack() as ar, nc.named_scope("PH3"):
        A = lambda n, sh, d: ar.enter_context(nc.sbuf_tensor(n, sh, d))
        t1b = A("p3t1", [128, T], BF16)
        t2b = A("p3t2", [128, T], BF16)
        t3b = A("p3t3", [128, T], BF16)
        qTb = [A(f"p3q{i}", [128, T], BF16) for i in range(2)]
        astg = [A(f"p3a{i}", [128, T], BF16) for i in range(2)]
        with nc.psum_tensor("p3pa", [128, 2048], F32) as pqA, \
             nc.psum_tensor("p3pb", [128, 2048], F32) as pqB, nc.Block() as blk:
            pq = [pqA, pqB]

            @blk.gpsimd
            def _(g):
                for h in range(2, HQ):
                    g.wait_ge(s.qmm.h, h - 1)
                    swq[h % 2].inc(g.dma_start(out=wkb[h % 2][:],
                                               in_=tq_d[h].rearrange("p i o -> p (i o)")), 16)

            @blk.sync
            def _(sy):
                for h in range(HQ):
                    sy.wait_ge(s.aev.h, 4 * h + 4)
                    saw[h % 2].inc(sy.dma_start(out=att_s[h], in_=astg[h % 2][:]), 16)

            @blk.tensor
            def _(te):
                def attn_mms(h):
                    te.wait_ge(s.phiq.h, h + 1)
                    kv = h // 4
                    for c in range(4):
                        s.amm.inc(te.matmul(pq[h % 2][:, c * 512:(c + 1) * 512],
                                            kvt_sb[:, kv * 128:(kv + 1) * 128],
                                            qTb[h % 2][:, c * 512:(c + 1) * 512],
                                            start=True, stop=True), 1)
                for h in range(HQ):
                    te.wait_ge(swq[h % 2].h, 16 * (h // 2 + 1))
                    if h >= 2:
                        te.wait_ge(s.aev.h, 4 * (h - 2) + 4)
                    pqh = pq[h % 2]
                    for i in range(NI):
                        if i == 8 and h >= 1:
                            attn_mms(h - 1)
                        for c in range(4):
                            ins = te.matmul(pqh[:, c * 512:(c + 1) * 512],
                                            wkb[h % 2][:, i * 128:(i + 1) * 128],
                                            bigv[:, i, c * 512:(c + 1) * 512],
                                            start=(i == 0), stop=(i == NI - 1))
                    s.qmm.inc(ins, 1)
                attn_mms(HQ - 1)

            @blk.vector
            def _(v):
                for h in range(HQ):
                    v.wait_ge(s.qmm.h, h + 1)
                    if h >= 1:
                        v.wait_ge(s.eq.h, h)
                    s.t1q.inc(v.tensor_scalar(t1b[:], pq[h % 2][:], bq_sb[:, h:h + 1], 0.0,
                                              op0=ALU.add, op1=ALU.min), 1)
                    v.tensor_scalar(t3b[:], pq[h % 2][:], bq_sb[:, h:h + 1], 0.0,
                                    op0=ALU.add, op1=ALU.max)
                    v.wait_ge(s.eq.h, h + 1)
                    if h >= 2:
                        v.wait_ge(s.amm.h, 4 * (h - 2) + 4)
                    s.phiq.inc(v.tensor_tensor(qTb[h % 2][:], t2b[:], t3b[:],
                                               op=ALU.add), 1)


            @blk.scalar
            def _(sc_e):
                sc_e.wait_ge(s.rcp.h, 1)
                for h in range(HQ):
                    if h >= 1:
                        ph = h - 1
                        kv = ph // 4
                        for c in range(4):
                            sc_e.wait_ge(s.amm.h, 4 * ph + c + 1)
                            if ph >= 2 and c == 0:
                                sc_e.wait_ge(saw[ph % 2].h, 16 * (ph // 2))
                            s.aev.inc(sc_e.activation(
                                astg[ph % 2][:, c * 512:(c + 1) * 512],
                                pq[ph % 2][:, c * 512:(c + 1) * 512],
                                AF.Copy, scale=recip[:, kv:kv + 1]), 1)
                    sc_e.wait_ge(s.t1q.h, h + 1)
                    if h >= 1:
                        sc_e.wait_ge(s.phiq.h, h)
                    s.eq.inc(sc_e.activation(t2b[:], t1b[:], AF.Exp), 1)
                ph = HQ - 1
                kv = ph // 4
                for c in range(4):
                    sc_e.wait_ge(s.amm.h, 4 * ph + c + 1)
                    sc_e.wait_ge(saw[ph % 2].h, 16 * (ph // 2))
                    s.aev.inc(sc_e.activation(
                        astg[ph % 2][:, c * 512:(c + 1) * 512],
                        pq[ph % 2][:, c * 512:(c + 1) * 512],
                        AF.Copy, scale=recip[:, kv:kv + 1]), 1)

    barrier()
    mid.close()

    with ExitStack() as ar, nc.named_scope("PH6"):
        A = lambda n, sh, d: ar.enter_context(nc.sbuf_tensor(n, sh, d))
        wob = [A(f"p6wo{i}", [128, (HQ // 2) * OC6], BF16) for i in range(4)]
        gin = [A(f"p6g{i}", [128, OC6], BF16) for i in range(2)]
        xin = [A(f"p6x{i}", [128, OC6], F32) for i in range(2)]
        ost = [A(f"p6o{i}", [128, OC6], F32) for i in range(2)]
        bo_oc = [A(f"p6bo{i}", [1, OC6], BF16) for i in range(2)]
        with ExitStack() as par, nc.Block() as blk:
            psl = [par.enter_context(nc.psum_tensor(f"p6s{i}", [128, 512], F32))
                   for i in range(8)]

            @blk.gpsimd
            def _(g):
                g.wait_ge(saw[0].h, 16 * (HQ // 2))
                g.wait_ge(saw[1].h, 16 * (HQ // 2))
                for b in range(2 * NOC6):
                    oc, hh = divmod(b, 2)
                    if b >= 4:
                        g.wait_ge(s.smm.h, NT * ((b - 4) // 2 + 1))
                    swo3[b % 4].inc(g.dma_start(
                        out=wob[b % 4][:],
                        in_=to_d[oc, hh].rearrange("p h o -> p (h o)")), 16)
                    if hh == 0:
                        if oc >= 2:
                            g.wait_ge(s.smm.h, NT * (oc - 1))
                        sbb[oc % 2].inc(g.dma_start(
                            out=bo_oc[oc % 2][:],
                            in_=bor_d[0:1, oc * OC6:(oc + 1) * OC6]), 16)

            @blk.sync
            def _(sy):
                sy.wait_ge(saw[0].h, 16 * (HQ // 2))
                sy.wait_ge(saw[1].h, 16 * (HQ // 2))
                for tq in range(4):
                    for h in range(16):
                        s.p5s.inc(sy.dma_start(
                            out=big[:, h * T + tq * 512:h * T + (tq + 1) * 512],
                            in_=att_s[h][:, tq * 512:(tq + 1) * 512]), 16)
                sy.wait_ge(sgst[0].h, 16 * (NK // 2))
                sy.wait_ge(sgst[1].h, 16 * (NK // 2))
                for k in range(NK6):
                    oc, tt = divmod(k, NT)
                    if k >= 2:
                        sy.wait_ge(s.oev.h, k - 1)
                    sgi[k % 2].inc(sy.dma_start(out=gin[k % 2][:, 0:OCW],
                                                in_=g_s[2 * oc, tt]), 16)
                    sgi[k % 2].inc(sy.dma_start(out=gin[k % 2][:, OCW:OC6],
                                                in_=g_s[2 * oc + 1, tt]), 16)
                    if k >= 1:
                        ko, (poc, ptt) = k - 1, divmod(k - 1, NT)
                        sy.wait_ge(s.oev.h, ko + 1)
                        sos[ko % 2].inc(sy.dma_start(
                            out=out_d[ptt * 128:(ptt + 1) * 128,
                                      poc * OC6:(poc + 1) * OC6],
                            in_=ost[ko % 2][:]), 16)
                ko, (poc, ptt) = NK6 - 1, divmod(NK6 - 1, NT)
                sy.wait_ge(s.oev.h, ko + 1)
                sos[ko % 2].inc(sy.dma_start(
                    out=out_d[ptt * 128:(ptt + 1) * 128, poc * OC6:(poc + 1) * OC6],
                    in_=ost[ko % 2][:]), 16)
                sy.wait_ge(sos[0].h, 16 * (NK6 // 2))
                sy.wait_ge(sos[1].h, 16 * (NK6 // 2))

            @blk.scalar
            def _(sc_e):
                sc_e.wait_ge(saw[0].h, 16 * (HQ // 2))
                sc_e.wait_ge(saw[1].h, 16 * (HQ // 2))
                for tq in range(4):
                    for h in range(16, HQ):
                        s.p5a.inc(sc_e.dma_start(
                            out=big[:, h * T + tq * 512:h * T + (tq + 1) * 512],
                            in_=att_s[h][:, tq * 512:(tq + 1) * 512]), 16)
                for k in range(NK6):
                    oc, tt = divmod(k, NT)
                    if k >= 2:
                        sc_e.wait_ge(s.oev.h, k - 1)
                    sxi[k % 2].inc(sc_e.dma_start(
                        out=xin[k % 2][:],
                        in_=x_d[tt * 128:(tt + 1) * 128, oc * OC6:(oc + 1) * OC6]), 16)

            @blk.tensor
            def _(te):
                for oc in range(NOC6):
                    te.wait_ge(sbb[oc % 2].h, 16 * (oc // 2 + 1))
                    for qd in range(NT // 4):
                        if oc == 0:
                            te.wait_ge(s.p5s.h, 256 * (qd + 1))
                            te.wait_ge(s.p5a.h, 256 * (qd + 1))
                        ks = [oc * NT + 4 * qd + j for j in range(4)]
                        if ks[0] >= 8:
                            te.wait_ge(s.oev.h, ks[0] - 4)
                        pq4 = [psl[k % 8] for k in ks]
                        for j in range(4):
                            te.matmul(pq4[j][:], ones_r[0:1, :],
                                      bo_oc[oc % 2][0:1, :], start=True, stop=False)
                        for h in range(HQ):
                            if qd == 0 and h % 16 == 0:
                                b = 2 * oc + h // 16
                                te.wait_ge(swo3[b % 4].h, 16 * (b // 4 + 1))
                            w_ap = wob[(2 * oc + h // 16) % 4][:,
                                       (h % 16) * OC6:(h % 16 + 1) * OC6]
                            for j in range(4):
                                tt = 4 * qd + j
                                ins = te.matmul(
                                    pq4[j][:],
                                    big[:, h * T + tt * 128: h * T + (tt + 1) * 128],
                                    w_ap, start=False, stop=(h == HQ - 1))
                                if h == HQ - 1:
                                    s.smm.inc(ins, 1)

            @blk.vector
            def _(v):
                for k in range(NK6):
                    v.wait_ge(s.smm.h, k + 1)
                    v.wait_ge(sgi[k % 2].h, 32 * (k // 2 + 1))
                    v.wait_ge(sxi[k % 2].h, 16 * (k // 2 + 1))
                    if k >= 2:
                        v.wait_ge(sos[k % 2].h, 16 * (k // 2))
                    v.tensor_tensor(ost[k % 2][:], psl[k % 8][:], gin[k % 2][:],
                                    op=ALU.mult)
                    s.oev.inc(v.tensor_tensor(ost[k % 2][:], ost[k % 2][:], xin[k % 2][:],
                                              op=ALU.add), 1)

    if DEBUG_HW:
        with nc.semaphore("s_dbgf") as sdbg, nc.Block() as blk:
            @blk.gpsimd
            def _(g):
                g.wait_ge(sos[0].h, 16 * (NK6 // 2))
                g.wait_ge(sos[1].h, 16 * (NK6 // 2))
                g.wait_ge(sdbgx.h, 16)
                g.dma_start(out=dbg_att, in_=att_s).then_inc(sdbg, 16)
                g.dma_start(out=dbg_g, in_=g_s).then_inc(sdbg, 16)
                g.dma_start(out=dbg_kvt, in_=kvt_sb[:]).then_inc(sdbg, 16)
                g.dma_start(out=dbg_nrm, in_=norms[:]).then_inc(sdbg, 16)
                g.dma_start(out=dbg_rcp, in_=recip[:]).then_inc(sdbg, 16)
                g.dma_start(out=dbg_big, in_=big[:]).then_inc(sdbg, 16)
                g.wait_ge(sdbg, 96)

    es.close()
    return nc


def kernel(**inputs):
    gi = lambda n: np.asarray(inputs[n])
    hs = gi("L0_attn_hidden_states").astype(np.float32, copy=False)
    Wq = gi("L0_attn_Wq").astype(np.float32, copy=False)
    bq = gi("L0_attn_bq").astype(np.float32, copy=False)
    Wk = gi("L0_attn_Wk").astype(np.float32, copy=False)
    bk = gi("L0_attn_bk").astype(np.float32, copy=False)
    Wv = gi("L0_attn_Wv").astype(np.float32, copy=False)
    bv = gi("L0_attn_bv").astype(np.float32, copy=False)
    Wo = gi("L0_attn_Wo").astype(np.float32, copy=False)
    bo = gi("L0_attn_bo").astype(np.float32, copy=False)
    pkw = gi("L0_ffn_packed_weight").astype(np.int32, copy=False)
    scl = gi("L0_ffn_scale").astype(np.float32, copy=False)
    fbi = gi("L0_ffn_bias").astype(np.float32, copy=False)

    TQ = np.ascontiguousarray(Wq.reshape(HQ, 128, NI, 128).transpose(0, 3, 2, 1))
    TK = np.ascontiguousarray(Wk.reshape(HKV, 128, NI, 128).transpose(0, 3, 2, 1))
    TV = np.ascontiguousarray(Wv.reshape(HKV, 128, NI, 128).transpose(0, 3, 2, 1))
    TO = np.ascontiguousarray(Wo.reshape(8, 512, 2, 16, 128).transpose(0, 2, 4, 3, 1))
    SC = np.ascontiguousarray(
        scl.reshape(NI, 128, 32).transpose(1, 0, 2).reshape(128, NI * 32))
    BQ = np.ascontiguousarray(bq.reshape(HQ, 128).T)
    BK = np.ascontiguousarray(bk.reshape(HKV, 128).T)
    BV = np.ascontiguousarray(bv.reshape(HKV, 128).T)
    BOR = np.ascontiguousarray(bo.reshape(1, D))
    FBR = np.ascontiguousarray(fbi.reshape(1, D))

    nc = build_program()
    in_maps = []
    for c in range(8):
        in_maps.append(dict(
            x=np.ascontiguousarray(hs[c]), tq=TQ, tk=TK, tv=TV, to=TO,
            pk=pkw, sc=SC, bq=BQ, bk=BK, bv=BV, bor=BOR, fbr=FBR,
        ))
    res = run_bass_kernel_spmd(nc, in_maps, core_ids=list(range(8)))
    return np.stack([r["out"] for r in res.results], axis=0)


if __name__ == "__main__":
    nc = build_program()
    print("instructions:", nc.n_instructions())
    print("sbuf remaining:", nc.sbuf_bytes_remaining)


# version 16
# speedup vs baseline: 1.1958x; 1.1958x over previous
import numpy as np
import ml_dtypes
from contextlib import ExitStack

import concourse.bass as bass
import concourse.mybir as mybir
from concourse.bass_utils import run_bass_kernel_spmd

F32 = mybir.dt.float32
BF16 = mybir.dt.float16
I32 = mybir.dt.int32
AF = mybir.ActivationFunctionType
ALU = mybir.AluOpType
AX = mybir.AxisListType.X

T = 2048
D = 4096
HKV = 8
HQ = 32
NT = T // 128
NI = D // 128
OCW = 256
NOC = D // OCW
NK = NOC * NT
OC6 = 512
NOC6 = D // OC6
NK6 = NOC6 * NT
NQ8 = 6
DEBUG_P2 = False
DEBUG_HW = False
LN_EPS = 1e-5
ATT_EPS = 1e-10


class Sem:
    def __init__(self, nc, es, name):
        self.h = es.enter_context(nc.semaphore(name))
        self.cnt = 0

    def inc(self, instr, k=1):
        instr.then_inc(self.h, k)
        self.cnt += k
        return instr


def build_program():
    nc = bass.Bass("TRN2", target_bir_lowering=False, detect_race_conditions=False)

    x_d = nc.dram_tensor("x", [T, D], F32, kind="ExternalInput").ap()
    tq_d = nc.dram_tensor("tq", [HQ, 128, NI, 128], F32, kind="ExternalInput").ap()
    tk_d = nc.dram_tensor("tk", [HKV, 128, NI, 128], F32, kind="ExternalInput").ap()
    tv_d = nc.dram_tensor("tv", [HKV, 128, NI, 128], F32, kind="ExternalInput").ap()
    to_d = nc.dram_tensor("to", [NOC6, 2, 128, HQ // 2, OC6], F32, kind="ExternalInput").ap()
    pk_d = nc.dram_tensor("pk", [D, 512], I32, kind="ExternalInput").ap()
    sc_d = nc.dram_tensor("sc", [128, NI * 32], F32, kind="ExternalInput").ap()
    bq_d = nc.dram_tensor("bq", [128, HQ], F32, kind="ExternalInput").ap()
    bk_d = nc.dram_tensor("bk", [128, HKV], F32, kind="ExternalInput").ap()
    bv_d = nc.dram_tensor("bv", [128, HKV], F32, kind="ExternalInput").ap()
    bor_d = nc.dram_tensor("bor", [1, D], F32, kind="ExternalInput").ap()
    fbr_d = nc.dram_tensor("fbr", [1, D], F32, kind="ExternalInput").ap()
    out_d = nc.dram_tensor("out", [T, D], F32, kind="ExternalOutput").ap()

    att_s = nc.dram_tensor("att_s", [HQ, 128, T], BF16, kind="Internal").ap()
    if DEBUG_HW:
        dbg_xnt = nc.dram_tensor("dbg_xnt", [128, NI * T], BF16, kind="ExternalOutput").ap()
        dbg_att = nc.dram_tensor("dbg_att", [HQ, 128, T], BF16, kind="ExternalOutput").ap()
        dbg_g = nc.dram_tensor("dbg_g", [NOC, NT, 128, OCW], BF16, kind="ExternalOutput").ap()
        dbg_kvt = nc.dram_tensor("dbg_kvt", [128, HKV * 128], BF16, kind="ExternalOutput").ap()
        dbg_nrm = nc.dram_tensor("dbg_nrm", [128, HKV], F32, kind="ExternalOutput").ap()
        dbg_rcp = nc.dram_tensor("dbg_rcp", [128, HKV], F32, kind="ExternalOutput").ap()
        dbg_big = nc.dram_tensor("dbg_big", [128, NI * T], BF16, kind="ExternalOutput").ap()
    if DEBUG_P2:
        dbg_kn = nc.dram_tensor("dbg_kn", [128, T], BF16, kind="Internal").ap()
        dbg_vn = nc.dram_tensor("dbg_vn", [128, T], BF16, kind="Internal").ap()
        dbg_kt = nc.dram_tensor("dbg_kt", [128, T], BF16, kind="Internal").ap()
        dbg_vt = nc.dram_tensor("dbg_vt", [128, T], BF16, kind="Internal").ap()
    g_s = nc.dram_tensor("g_s", [NOC, NT, 128, OCW], BF16, kind="Internal").ap()

    ident_d = nc.inline_tensor(np.eye(128, dtype=np.float16), "ident_c").ap()
    ones_d = nc.inline_tensor(np.ones((1, 128), dtype=np.float16), "ones_c").ap()

    es = ExitStack()
    SB = lambda n, sh, d: es.enter_context(nc.sbuf_tensor(n, sh, d))

    big = SB("big", [128, NI * T], BF16)
    norms = SB("norms", [128, HKV], F32)
    recip = SB("recip", [128, HKV], F32)
    bq_sb = SB("bq_sb", [128, HQ], F32)
    bk_sb = SB("bk_sb", [128, HKV], F32)
    bv_sb = SB("bv_sb", [128, HKV], F32)
    ident = SB("ident", [128, 128], BF16)
    ones_r = SB("ones_r", [1, 128], BF16)
    stat = SB("stat", [128, 64], F32)
    junkA = SB("junkA", [128, 128], BF16)
    junkB = SB("junkB", [128, 128], BF16)
    mid = ExitStack()
    wkb = [mid.enter_context(nc.sbuf_tensor(f"wkb{i}", [128, NI * 128], BF16))
           for i in range(2)]
    sc_sb = mid.enter_context(nc.sbuf_tensor("sc_sb", [128, NI * 32], F32))
    kvt_sb = mid.enter_context(nc.sbuf_tensor("kvt_sb", [128, HKV * 128], BF16))

    bigv = big.rearrange("p (a t) -> p a t", a=NI)

    names = ("bar", "cst", "stats", "var", "rstd", "xn", "tr", "trev",
             "mmk", "mmv", "t1k", "ek", "phik", "cpv",
             "trk", "trkev", "trv", "trvev", "kvt", "kvtev", "nrm", "rcp",
             "deq", "trw", "trwev", "gmm", "gev",
             "qmm", "t1q", "eq", "phiq", "amm", "aev",
             "p5g", "p5s", "p5a", "smm", "oev", "m1", "m2", "fbmm", "fbev", "gvb")
    sems = {n: Sem(nc, es, "s_" + n) for n in names}
    s = type("S", (), sems)
    P = lambda n: [Sem(nc, es, f"s_{n}{i}") for i in range(2)]
    skk, skv, spk, sgst = P("kk"), P("kv"), P("pk"), P("gst")
    sxb = [Sem(nc, es, f"s_xb{i}") for i in range(4)]
    swq, saw, swo, sfb, sbb = P("wq"), P("aw"), P("wo"), P("fb"), P("bb")
    swo3 = [Sem(nc, es, f"s_w3{i}") for i in range(4)]
    sgi, sxi, sos = P("gi"), P("xi"), P("os")
    sdbgx = Sem(nc, es, "s_dbgx")

    NENG = 5
    bar_n = [0]

    def barrier():
        bar_n[0] += 1
        tgt = NENG * bar_n[0]
        with nc.Block() as blk:
            for name in ("sync", "gpsimd", "tensor", "vector", "scalar"):
                def body(e, tgt=tgt):
                    s.bar.cnt += 1
                    e.nop().then_inc(s.bar.h, 1)
                    e.wait_ge(s.bar.h, tgt)
                getattr(blk, name)(body)

    with nc.Block() as blk:
        @blk.vector
        def _(v):
            v.memset(junkA[:], 0.0)

        @blk.gpsimd
        def _(g):
            s.cst.inc(g.dma_start(out=ident[:], in_=ident_d), 16)
            s.cst.inc(g.dma_start(out=ones_r[:], in_=ones_d), 16)
            s.cst.inc(g.dma_start(out=bq_sb[:], in_=bq_d), 16)
            s.cst.inc(g.dma_start(out=bk_sb[:], in_=bk_d), 16)
            s.cst.inc(g.dma_start(out=bv_sb[:], in_=bv_d), 16)
            s.cst.inc(g.dma_start(out=sc_sb[:], in_=sc_d), 16)

    with ExitStack() as ar, nc.named_scope("PH1"):
        A = lambda n, sh, d: ar.enter_context(nc.sbuf_tensor(n, sh, d))
        xb = [A(f"p1x{i}", [128, D], BF16) for i in range(4)]
        xnb = [A(f"p1xn{i}", [128, D], BF16) for i in range(2)]
        bnst = [A(f"p1bn{i}", [128, 48], F32) for i in range(2)]
        with nc.psum_tensor("p1ps", [128, 4096], BF16) as ps, nc.Block() as blk:

            @blk.gpsimd
            def _(g):
                for tt in range(NT):
                    if tt >= 4:
                        g.wait_ge(s.xn.h, tt - 3)
                    sxb[tt % 4].inc(g.dma_start(out=xb[tt % 4][:],
                                                in_=x_d[tt * 128:(tt + 1) * 128, :]), 16)
                    if tt == 2:
                        skk[0].inc(g.dma_start(
                            out=wkb[0][:], in_=tk_d[0].rearrange("p i o -> p (i o)")), 16)

            @blk.scalar
            def _(sc_e):
                def rstd_step(pt):
                    psl = (pt % 4) * 16
                    sc_e.wait_ge(s.var.h, pt + 1)
                    s.rstd.inc(sc_e.activation(stat[:, psl + 4:psl + 5],
                                               stat[:, psl + 3:psl + 4], AF.Sqrt), 1)

                def evacs(pt):
                    for b in range(4):
                        sc_e.wait_ge(s.tr.h, 4 * pt + b + 1)
                        s.trev.inc(sc_e.activation(
                            bigv[:, b * 8:(b + 1) * 8, pt * 128:(pt + 1) * 128],
                            ps[:, b * 1024:(b + 1) * 1024], AF.Copy), 1)

                for tt in range(NT):
                    rstd_step(tt)
                    if tt >= 2:
                        evacs(tt - 2)
                evacs(NT - 2)
                evacs(NT - 1)

            @blk.vector
            def _(v):
                def spacer():
                    v.tensor_copy(junkB[:], junkA[:])
                    v.tensor_copy(junkA[:], junkB[:])

                def seg_a(tt):
                    v.wait_ge(sxb[tt % 4].h, 16 * (tt // 4 + 1))
                    for i in range(8):
                        v.bn_stats(bnst[tt % 2][:, 6 * i:6 * (i + 1)],
                                   xb[tt % 4][:, 512 * i:512 * (i + 1)])

                def seg_b(tt):
                    sl = (tt % 4) * 16
                    v.bn_aggr(stat[:, sl:sl + 2], bnst[tt % 2][:])
                    spacer()
                    v.tensor_scalar(stat[:, sl + 2:sl + 3], stat[:, sl + 1:sl + 2],
                                    LN_EPS, None, op0=ALU.add)
                    spacer()
                    s.var.inc(v.reciprocal(stat[:, sl + 3:sl + 4],
                                           stat[:, sl + 2:sl + 3]), 1)

                def seg_c(pt):
                    v.wait_ge(s.rstd.h, pt + 1)
                    sl = (pt % 4) * 16
                    v.tensor_scalar(stat[:, sl + 5:sl + 6], stat[:, sl:sl + 1],
                                    stat[:, sl + 4:sl + 5], -1.0,
                                    op0=ALU.mult, op1=ALU.mult)
                    spacer()
                    if pt >= 2:
                        v.wait_ge(s.tr.h, 4 * (pt - 1))
                    s.xn.inc(v.tensor_scalar(xnb[pt % 2][:], xb[pt % 4][:],
                                             stat[:, sl + 4:sl + 5], stat[:, sl + 5:sl + 6],
                                             op0=ALU.mult, op1=ALU.add), 1)

                for t in range(NT):
                    seg_a(t)
                    if t >= 1:
                        seg_c(t - 1)
                    else:
                        spacer()
                        spacer()
                    seg_b(t)
                seg_c(NT - 1)

            @blk.tensor
            def _(te):
                te.wait_ge(s.cst.h, 96)
                for tt in range(NT):
                    te.wait_ge(s.xn.h, tt + 1)
                    if tt >= 1:
                        te.wait_ge(s.trev.h, 4 * tt)
                    for j in range(NI):
                        b, jj = j // 8, j % 8
                        ins = te.transpose(
                            ps[:, b * 1024 + jj * 128: b * 1024 + (jj + 1) * 128],
                            xnb[tt % 2][:, j * 128:(j + 1) * 128], ident[:])
                        if jj == 7:
                            s.tr.inc(ins, 1)

    barrier()

    with ExitStack() as ar, nc.named_scope("PH2"):
        A = lambda n, sh, d: ar.enter_context(nc.sbuf_tensor(n, sh, d))
        wvb = [A(f"p2wv{i}", [128, NI * 128], BF16) for i in range(2)]
        t1b = A("p2t1", [128, T], BF16)
        t2b = A("p2t2", [128, T], BF16)
        t3b = A("p2t3", [128, T], BF16)
        kTb = [A(f"p2kT{i}", [128, T], BF16) for i in range(2)]
        vTb = [A(f"p2vT{i}", [128, T], BF16) for i in range(2)]
        k_nat = A("p2kn", [128, T], BF16)
        v_nat = A("p2vn", [128, T], BF16)
        with nc.psum_tensor("p2pk", [128, 2048], F32) as pk_ps, \
             nc.psum_tensor("p2pv", [128, 2048], F32) as pv_ps, nc.Block() as blk:
            pk_bf = pk_ps[:].bitcast(BF16)

            @blk.gpsimd
            def _(g):
                skv[0].inc(g.dma_start(out=wvb[0][:],
                                       in_=tv_d[0].rearrange("p i o -> p (i o)")), 16)
                for h in range(1, HKV):
                    if h >= 2:
                        g.wait_ge(s.mmk.h, h - 1)
                    skk[h % 2].inc(g.dma_start(out=wkb[h % 2][:],
                                               in_=tk_d[h].rearrange("p i o -> p (i o)")), 16)
                    if h >= 2:
                        g.wait_ge(s.mmv.h, h - 1)
                    skv[h % 2].inc(g.dma_start(out=wvb[h % 2][:],
                                               in_=tv_d[h].rearrange("p i o -> p (i o)")), 16)
                if DEBUG_P2:
                    g.wait_ge(s.kvtev.h, HKV)
                    g.dma_start(out=dbg_kn, in_=k_nat[:])
                    g.dma_start(out=dbg_vn, in_=v_nat[:])
                    g.dma_start(out=dbg_kt, in_=kTb[1][:])
                    g.dma_start(out=dbg_vt, in_=vTb[1][:])

            @blk.tensor
            def _(te):
                def kvt_mms(ph):
                    te.wait_ge(s.trkev.h, 2 * ph + 2)
                    te.wait_ge(s.trvev.h, 2 * ph + 2)
                    for tt in range(NT):
                        ins = te.matmul(pv_ps[:, 0:128],
                                        k_nat[:, tt * 128:(tt + 1) * 128],
                                        v_nat[:, tt * 128:(tt + 1) * 128],
                                        start=(tt == 0), stop=(tt == NT - 1))
                    s.kvt.inc(ins, 1)
                for h in range(HKV):
                    te.wait_ge(skk[h % 2].h, 16 * (h // 2 + 1))
                    for c in range(4):
                        if h >= 1:
                            if c < 2:
                                te.wait_ge(s.trkev.h, 2 * (h - 1) + c + 1)
                            else:
                                te.wait_ge(s.trvev.h, 2 * (h - 1) + c - 1)
                        if c == 3 and h >= 1:
                            kvt_mms(h - 1)
                        for i in range(NI):
                            ins = te.matmul(pk_ps[:, c * 512:(c + 1) * 512],
                                            wkb[h % 2][:, i * 128:(i + 1) * 128],
                                            bigv[:, i, c * 512:(c + 1) * 512],
                                            start=(i == 0), stop=(i == NI - 1))
                    s.mmk.inc(ins, 1)
                    if h == HKV - 1:
                        pass
                    te.wait_ge(skv[h % 2].h, 16 * (h // 2 + 1))
                    if h >= 1:
                        te.wait_ge(s.kvtev.h, h)
                    for i in range(NI):
                        for c in range(4):
                            ins = te.matmul(pv_ps[:, c * 512:(c + 1) * 512],
                                            wvb[h % 2][:, i * 128:(i + 1) * 128],
                                            bigv[:, i, c * 512:(c + 1) * 512],
                                            start=(i == 0), stop=(i == NI - 1))
                    s.mmv.inc(ins, 1)
                    te.wait_ge(s.phik.h, h + 1)
                    for tt in range(NT):
                        b, jj = tt // 8, tt % 8
                        ins = te.transpose(
                            pk_bf[:, b * 1024 + jj * 128: b * 1024 + (jj + 1) * 128],
                            kTb[h % 2][:, tt * 128:(tt + 1) * 128], ident[:])
                        if jj == 7:
                            s.trk.inc(ins, 1)
                    te.wait_ge(s.cpv.h, h + 1)
                    for tt in range(NT):
                        b, jj = 2 + tt // 8, tt % 8
                        ins = te.transpose(
                            pk_bf[:, b * 1024 + jj * 128: b * 1024 + (jj + 1) * 128],
                            vTb[h % 2][:, tt * 128:(tt + 1) * 128], ident[:])
                        if jj == 7:
                            s.trv.inc(ins, 1)
                    if h == HKV - 1:
                        kvt_mms(h)

            @blk.vector
            def _(v):
                for h in range(HKV):
                    v.wait_ge(s.mmk.h, h + 1)
                    if h >= 1:
                        v.wait_ge(s.ek.h, h)
                    if h >= 2:
                        v.wait_ge(s.trk.h, 2 * (h - 1))
                    s.t1k.inc(v.tensor_scalar(t1b[:], pk_ps[:], bk_sb[:, h:h + 1], 0.0,
                                              op0=ALU.add, op1=ALU.min), 1)
                    v.tensor_scalar(t3b[:], pk_ps[:], bk_sb[:, h:h + 1], 0.0,
                                    op0=ALU.add, op1=ALU.max)
                    v.wait_ge(s.ek.h, h + 1)
                    s.phik.inc(v.tensor_tensor(kTb[h % 2][:], t2b[:], t3b[:],
                                               op=ALU.add), 1)
                    v.wait_ge(s.mmv.h, h + 1)
                    if h >= 2:
                        v.wait_ge(s.trv.h, 2 * (h - 1))
                    s.cpv.inc(v.tensor_scalar(vTb[h % 2][:], pv_ps[:],
                                              bv_sb[:, h:h + 1], None,
                                              op0=ALU.add), 1)
                    s.nrm.inc(v.reduce_sum(norms[:, h:h + 1], kTb[h % 2][:], axis=AX), 1)
                    for b in range(2):
                        v.wait_ge(s.trk.h, 2 * h + b + 1)
                        if h >= 1 and b == 0:
                            v.wait_ge(s.kvt.h, h)
                        s.trkev.inc(v.tensor_copy(k_nat[:, b * 1024:(b + 1) * 1024],
                                                  pk_bf[:, b * 1024:(b + 1) * 1024]), 1)
                    for b in range(2):
                        v.wait_ge(s.trv.h, 2 * h + b + 1)
                        if h >= 1 and b == 0:
                            v.wait_ge(s.kvt.h, h)
                        s.trvev.inc(v.tensor_copy(v_nat[:, b * 1024:(b + 1) * 1024],
                                                  pk_bf[:, (2 + b) * 1024:(3 + b) * 1024]), 1)
                    v.wait_ge(s.kvt.h, h + 1)
                    s.kvtev.inc(v.tensor_copy(kvt_sb[:, h * 128:(h + 1) * 128],
                                              pv_ps[:, 0:128]), 1)
                v.wait_ge(s.nrm.h, HKV)
                v.tensor_scalar(stat[:, 0:HKV], norms[:], ATT_EPS, None, op0=ALU.add)
                v.tensor_copy(junkB[:], junkA[:])
                v.tensor_copy(junkA[:], junkB[:])
                s.rcp.inc(v.reciprocal(recip[:], stat[:, 0:HKV]), 1)

            @blk.scalar
            def _(sc_e):
                for h in range(HKV):
                    sc_e.wait_ge(s.t1k.h, h + 1)
                    if h >= 1:
                        sc_e.wait_ge(s.phik.h, h)
                    s.ek.inc(sc_e.activation(t2b[:], t1b[:], AF.Exp), 1)

    barrier()

    with ExitStack() as ar, nc.named_scope("PH4"):
        A = lambda n, sh, d: ar.enter_context(nc.sbuf_tensor(n, sh, d))
        WT = [A(f"p4wt{i}", [128, NI * OCW], BF16) for i in range(2)]
        pkb = [A(f"p4pk{i}", [128, 512], I32) for i in range(2)]
        wi32 = A("p4wi", [128, 1024], I32)
        wq8 = [A(f"p4w8{i}", [128, 1024], BF16) for i in range(NQ8)]
        gst = [A(f"p4gs{i}", [128, OCW], BF16) for i in range(2)]
        fb_oc = [A(f"p4fb{i}", [1, OCW], BF16) for i in range(2)]
        fbb = [A(f"p4fbb{i}", [128, OCW], BF16) for i in range(2)]
        gtmp = [A(f"p4gt{i}", [128, OCW], BF16) for i in range(2)]
        with nc.psum_tensor("p4ga", [128, 512], F32) as pgA, \
             nc.psum_tensor("p4gb", [128, 512], F32) as pgB, \
             nc.psum_tensor("p4ta", [128, 1024], BF16) as ptA, \
             nc.psum_tensor("p4tb", [128, 1024], BF16) as ptB, \
             nc.psum_tensor("p4bc", [128, 512], F32) as pbc, nc.Block() as blk:
            pg = [pgA, pgB]
            ptr = [ptA, ptB]

            @blk.gpsimd
            def _(g):
                if DEBUG_HW:
                    sdbgx.inc(g.dma_start(out=dbg_xnt, in_=big[:]), 16)
                swq[0].inc(g.dma_start(out=wkb[0][:],
                                       in_=tq_d[0].rearrange("p i o -> p (i o)")), 16)
                swq[1].inc(g.dma_start(out=wkb[1][:],
                                       in_=tq_d[1].rearrange("p i o -> p (i o)")), 16)
                for oc in range(NOC):
                    if oc >= 2:
                        g.wait_ge(s.gmm.h, NT * (oc - 1))
                    sfb[oc % 2].inc(g.dma_start(out=fb_oc[oc % 2][:],
                                                in_=fbr_d[0:1, oc * OCW:(oc + 1) * OCW]), 16)

            @blk.sync
            def _(sy):
                def packed(ptg):
                    if ptg >= 2:
                        sy.wait_ge(s.deq.h, 4 * (ptg - 1))
                    spk[ptg % 2].inc(sy.dma_start(out=pkb[ptg % 2][:],
                                                  in_=pk_d[ptg * 128:(ptg + 1) * 128, :]), 16)
                for ptg in range(4):
                    packed(ptg)
                for oc in range(NOC):
                    for ptg in (2 * oc + 4, 2 * oc + 5):
                        if ptg < NI:
                            packed(ptg)
                    for tt in range(NT):
                        k = oc * NT + tt
                        sy.wait_ge(s.gev.h, k + 1)
                        sgst[k % 2].inc(sy.dma_start(out=g_s[oc, tt],
                                                     in_=gst[k % 2][:]), 16)

            @blk.vector
            def _(v):
                kq_ctr = [0]

                def gvb_step(v, kq):
                    oc_b = kq // NT
                    if kq % NT == 0:
                        v.wait_ge(s.fbmm.h, oc_b + 1)
                        if oc_b >= 2:
                            v.wait_ge(s.gvb.h, NT * (oc_b - 1))
                        s.fbev.inc(v.tensor_copy(fbb[oc_b % 2][:],
                                                 pbc[:, 0:OCW]), 1)
                    v.wait_ge(s.gmm.h, kq + 1)
                    if kq >= 2:
                        v.wait_ge(s.gev.h, kq - 1)
                    s.gvb.inc(v.tensor_tensor(
                        gtmp[kq % 2][:], pg[kq % 2][:, 0:OCW],
                        fbb[oc_b % 2][:], op=ALU.add), 1)

                def evac(j):
                    ptg2, iq2 = divmod(j, 4)
                    oc2, ptl2 = divmod(ptg2, 2)
                    if j % 8 == 0 and oc2 >= 2:
                        v.wait_ge(s.gmm.h, NT * (oc2 - 1))
                    dst = WT[oc2 % 2].rearrange("p (i o) -> p i o", i=NI)[
                        :, iq2 * 8:(iq2 + 1) * 8, ptl2 * 128:(ptl2 + 1) * 128]
                    s.trwev.inc(v.tensor_copy(dst, ptr[j % 2][:]), 1)
                for ptg in range(NI):
                    v.wait_ge(spk[ptg % 2].h, 16 * (ptg // 2 + 1))
                    for iq in range(4):
                        qk = 4 * ptg + iq
                        if qk >= NQ8:
                            v.wait_ge(s.trw.h, qk - NQ8 + 1)
                        for e in range(8):
                            v.tensor_scalar(wi32[:, e::8],
                                            pkb[ptg % 2][:, iq * 128:(iq + 1) * 128],
                                            4 * e, 28,
                                            op0=ALU.arith_shift_left,
                                            op1=ALU.arith_shift_right)
                        for gl in range(8):
                            gg = iq * 8 + gl
                            ins = v.tensor_scalar(wq8[qk % NQ8][:, gl * 128:(gl + 1) * 128],
                                                  wi32[:, gl * 128:(gl + 1) * 128],
                                                  sc_sb[:, ptg * 32 + gg: ptg * 32 + gg + 1],
                                                  None, op0=ALU.mult)
                        s.deq.inc(ins, 1)
                        if qk >= NQ8:
                            evac(qk - NQ8)
                        if qk >= 14:
                            for _ in range(2):
                                kq = kq_ctr[0]
                                if kq < NK:
                                    gvb_step(v, kq)
                                    kq_ctr[0] += 1
                for j in range(4 * NI - NQ8, 4 * NI):
                    target = min(NK, 16 * (j // 8 - 1) + 2 * (j % 8) + 4,
                                 16 * (j // 8))
                    while kq_ctr[0] < target:
                        gvb_step(v, kq_ctr[0])
                        kq_ctr[0] += 1
                    v.wait_ge(s.trw.h, j + 1)
                    evac(j)
                while kq_ctr[0] < NK:
                    gvb_step(v, kq_ctr[0])
                    kq_ctr[0] += 1

            @blk.tensor
            def _(te):
                def transposes(ptg):
                    for iq in range(4):
                        qk = 4 * ptg + iq
                        te.wait_ge(s.deq.h, qk + 1)
                        if qk >= 2:
                            te.wait_ge(s.trwev.h, qk - 1)
                        for il in range(8):
                            ins = te.transpose(ptr[qk % 2][:, il * 128:(il + 1) * 128],
                                               wq8[qk % NQ8][:, il * 128:(il + 1) * 128],
                                               ident[:])
                        s.trw.inc(ins, 1)
                def tr_quarter(qk):
                    te.wait_ge(s.deq.h, qk + 1)
                    if qk >= 2:
                        te.wait_ge(s.trwev.h, qk - 1)
                    for il in range(8):
                        ins = te.transpose(ptr[qk % 2][:, il * 128:(il + 1) * 128],
                                           wq8[qk % NQ8][:, il * 128:(il + 1) * 128],
                                           ident[:])
                    s.trw.inc(ins, 1)
                for qk in range(8):
                    tr_quarter(qk)
                for oc in range(NOC):
                    te.wait_ge(s.trwev.h, 8 * (oc + 1))
                    te.wait_ge(sfb[oc % 2].h, 16 * (oc // 2 + 1))
                    if oc >= 1:
                        te.wait_ge(s.fbev.h, oc)
                    s.fbmm.inc(te.matmul(pbc[:, 0:OCW], ones_r[0:1, :],
                                         fb_oc[oc % 2][0:1, :],
                                         start=True, stop=True), 1)
                    for tt in range(NT):
                        k = oc * NT + tt
                        if tt % 2 == 0:
                            qk = 8 * (oc + 1) + tt // 2
                            if qk < 4 * NI:
                                tr_quarter(qk)
                        if k >= 2:
                            te.wait_ge(s.gev.h, k - 1)
                        pgk = pg[k % 2]
                        for i in range(NI):
                            ins = te.matmul(pgk[:, 0:OCW],
                                            bigv[:, i, tt * 128:(tt + 1) * 128],
                                            WT[oc % 2][:, i * OCW:(i + 1) * OCW],
                                            start=(i == 0), stop=(i == NI - 1))
                        s.gmm.inc(ins, 1)

            @blk.scalar
            def _(sc_e):
                for k in range(NK):
                    sc_e.wait_ge(s.gvb.h, k + 1)
                    if k >= 2:
                        sc_e.wait_ge(sgst[k % 2].h, 16 * (k // 2))
                    s.gev.inc(sc_e.activation(gst[k % 2][:], gtmp[k % 2][:],
                                              AF.Silu), 1)

    barrier()

    with ExitStack() as ar, nc.named_scope("PH3"):
        A = lambda n, sh, d: ar.enter_context(nc.sbuf_tensor(n, sh, d))
        t1b = A("p3t1", [128, T], BF16)
        t2b = A("p3t2", [128, T], BF16)
        t3b = A("p3t3", [128, T], BF16)
        qTb = [A(f"p3q{i}", [128, T], BF16) for i in range(2)]
        astg = [A(f"p3a{i}", [128, T], BF16) for i in range(2)]
        with nc.psum_tensor("p3pa", [128, 2048], F32) as pqA, \
             nc.psum_tensor("p3pb", [128, 2048], F32) as pqB, nc.Block() as blk:
            pq = [pqA, pqB]

            @blk.gpsimd
            def _(g):
                for h in range(2, HQ):
                    g.wait_ge(s.qmm.h, h - 1)
                    swq[h % 2].inc(g.dma_start(out=wkb[h % 2][:],
                                               in_=tq_d[h].rearrange("p i o -> p (i o)")), 16)

            @blk.sync
            def _(sy):
                for h in range(HQ):
                    sy.wait_ge(s.aev.h, 4 * h + 4)
                    saw[h % 2].inc(sy.dma_start(out=att_s[h], in_=astg[h % 2][:]), 16)

            @blk.tensor
            def _(te):
                def attn_mms(h):
                    te.wait_ge(s.phiq.h, h + 1)
                    kv = h // 4
                    for c in range(4):
                        s.amm.inc(te.matmul(pq[h % 2][:, c * 512:(c + 1) * 512],
                                            kvt_sb[:, kv * 128:(kv + 1) * 128],
                                            qTb[h % 2][:, c * 512:(c + 1) * 512],
                                            start=True, stop=True), 1)
                for h in range(HQ):
                    te.wait_ge(swq[h % 2].h, 16 * (h // 2 + 1))
                    if h >= 2:
                        te.wait_ge(s.aev.h, 4 * (h - 2) + 4)
                    pqh = pq[h % 2]
                    for i in range(NI):
                        if i == 8 and h >= 1:
                            attn_mms(h - 1)
                        for c in range(4):
                            ins = te.matmul(pqh[:, c * 512:(c + 1) * 512],
                                            wkb[h % 2][:, i * 128:(i + 1) * 128],
                                            bigv[:, i, c * 512:(c + 1) * 512],
                                            start=(i == 0), stop=(i == NI - 1))
                    s.qmm.inc(ins, 1)
                attn_mms(HQ - 1)

            @blk.vector
            def _(v):
                for h in range(HQ):
                    v.wait_ge(s.qmm.h, h + 1)
                    if h >= 1:
                        v.wait_ge(s.eq.h, h)
                    s.t1q.inc(v.tensor_scalar(t1b[:], pq[h % 2][:], bq_sb[:, h:h + 1], 0.0,
                                              op0=ALU.add, op1=ALU.min), 1)
                    v.tensor_scalar(t3b[:], pq[h % 2][:], bq_sb[:, h:h + 1], 0.0,
                                    op0=ALU.add, op1=ALU.max)
                    v.wait_ge(s.eq.h, h + 1)
                    if h >= 2:
                        v.wait_ge(s.amm.h, 4 * (h - 2) + 4)
                    s.phiq.inc(v.tensor_tensor(qTb[h % 2][:], t2b[:], t3b[:],
                                               op=ALU.add), 1)


            @blk.scalar
            def _(sc_e):
                sc_e.wait_ge(s.rcp.h, 1)
                for h in range(HQ):
                    if h >= 1:
                        ph = h - 1
                        kv = ph // 4
                        for c in range(4):
                            sc_e.wait_ge(s.amm.h, 4 * ph + c + 1)
                            if ph >= 2 and c == 0:
                                sc_e.wait_ge(saw[ph % 2].h, 16 * (ph // 2))
                            s.aev.inc(sc_e.activation(
                                astg[ph % 2][:, c * 512:(c + 1) * 512],
                                pq[ph % 2][:, c * 512:(c + 1) * 512],
                                AF.Copy, scale=recip[:, kv:kv + 1]), 1)
                    sc_e.wait_ge(s.t1q.h, h + 1)
                    if h >= 1:
                        sc_e.wait_ge(s.phiq.h, h)
                    s.eq.inc(sc_e.activation(t2b[:], t1b[:], AF.Exp), 1)
                ph = HQ - 1
                kv = ph // 4
                for c in range(4):
                    sc_e.wait_ge(s.amm.h, 4 * ph + c + 1)
                    sc_e.wait_ge(saw[ph % 2].h, 16 * (ph // 2))
                    s.aev.inc(sc_e.activation(
                        astg[ph % 2][:, c * 512:(c + 1) * 512],
                        pq[ph % 2][:, c * 512:(c + 1) * 512],
                        AF.Copy, scale=recip[:, kv:kv + 1]), 1)

    barrier()
    mid.close()

    with ExitStack() as ar, nc.named_scope("PH6"):
        A = lambda n, sh, d: ar.enter_context(nc.sbuf_tensor(n, sh, d))
        wob = [A(f"p6wo{i}", [128, (HQ // 2) * OC6], BF16) for i in range(4)]
        gin = [A(f"p6g{i}", [128, OC6], BF16) for i in range(2)]
        xin = [A(f"p6x{i}", [128, OC6], F32) for i in range(2)]
        ost = [A(f"p6o{i}", [128, OC6], F32) for i in range(2)]
        bo_oc = [A(f"p6bo{i}", [1, OC6], BF16) for i in range(2)]
        with ExitStack() as par, nc.Block() as blk:
            psl = [par.enter_context(nc.psum_tensor(f"p6s{i}", [128, 512], F32))
                   for i in range(8)]

            @blk.gpsimd
            def _(g):
                g.wait_ge(saw[0].h, 16 * (HQ // 2))
                g.wait_ge(saw[1].h, 16 * (HQ // 2))
                for b in range(2 * NOC6):
                    oc, hh = divmod(b, 2)
                    if b >= 4:
                        g.wait_ge(s.smm.h, NT * ((b - 4) // 2 + 1))
                    swo3[b % 4].inc(g.dma_start(
                        out=wob[b % 4][:],
                        in_=to_d[oc, hh].rearrange("p h o -> p (h o)")), 16)
                    if hh == 0:
                        if oc >= 2:
                            g.wait_ge(s.smm.h, NT * (oc - 1))
                        sbb[oc % 2].inc(g.dma_start(
                            out=bo_oc[oc % 2][:],
                            in_=bor_d[0:1, oc * OC6:(oc + 1) * OC6]), 16)

            @blk.sync
            def _(sy):
                sy.wait_ge(saw[0].h, 16 * (HQ // 2))
                sy.wait_ge(saw[1].h, 16 * (HQ // 2))
                for tq in range(4):
                    for h in range(16):
                        s.p5s.inc(sy.dma_start(
                            out=big[:, h * T + tq * 512:h * T + (tq + 1) * 512],
                            in_=att_s[h][:, tq * 512:(tq + 1) * 512]), 16)
                sy.wait_ge(sgst[0].h, 16 * (NK // 2))
                sy.wait_ge(sgst[1].h, 16 * (NK // 2))
                for k in range(NK6):
                    oc, tt = divmod(k, NT)
                    if k >= 2:
                        sy.wait_ge(s.oev.h, k - 1)
                    sgi[k % 2].inc(sy.dma_start(out=gin[k % 2][:, 0:OCW],
                                                in_=g_s[2 * oc, tt]), 16)
                    sgi[k % 2].inc(sy.dma_start(out=gin[k % 2][:, OCW:OC6],
                                                in_=g_s[2 * oc + 1, tt]), 16)
                    if k >= 1:
                        ko, (poc, ptt) = k - 1, divmod(k - 1, NT)
                        sy.wait_ge(s.oev.h, ko + 1)
                        sos[ko % 2].inc(sy.dma_start(
                            out=out_d[ptt * 128:(ptt + 1) * 128,
                                      poc * OC6:(poc + 1) * OC6],
                            in_=ost[ko % 2][:]), 16)
                ko, (poc, ptt) = NK6 - 1, divmod(NK6 - 1, NT)
                sy.wait_ge(s.oev.h, ko + 1)
                sos[ko % 2].inc(sy.dma_start(
                    out=out_d[ptt * 128:(ptt + 1) * 128, poc * OC6:(poc + 1) * OC6],
                    in_=ost[ko % 2][:]), 16)
                sy.wait_ge(sos[0].h, 16 * (NK6 // 2))
                sy.wait_ge(sos[1].h, 16 * (NK6 // 2))

            @blk.scalar
            def _(sc_e):
                sc_e.wait_ge(saw[0].h, 16 * (HQ // 2))
                sc_e.wait_ge(saw[1].h, 16 * (HQ // 2))
                for tq in range(4):
                    for h in range(16, HQ):
                        s.p5a.inc(sc_e.dma_start(
                            out=big[:, h * T + tq * 512:h * T + (tq + 1) * 512],
                            in_=att_s[h][:, tq * 512:(tq + 1) * 512]), 16)
                for k in range(NK6):
                    oc, tt = divmod(k, NT)
                    if k >= 2:
                        sc_e.wait_ge(s.oev.h, k - 1)
                    sxi[k % 2].inc(sc_e.dma_start(
                        out=xin[k % 2][:],
                        in_=x_d[tt * 128:(tt + 1) * 128, oc * OC6:(oc + 1) * OC6]), 16)

            @blk.tensor
            def _(te):
                for oc in range(NOC6):
                    te.wait_ge(sbb[oc % 2].h, 16 * (oc // 2 + 1))
                    for qd in range(NT // 4):
                        if oc == 0:
                            te.wait_ge(s.p5s.h, 256 * (qd + 1))
                            te.wait_ge(s.p5a.h, 256 * (qd + 1))
                        ks = [oc * NT + 4 * qd + j for j in range(4)]
                        if ks[0] >= 8:
                            te.wait_ge(s.oev.h, ks[0] - 4)
                        pq4 = [psl[k % 8] for k in ks]
                        for j in range(4):
                            te.matmul(pq4[j][:], ones_r[0:1, :],
                                      bo_oc[oc % 2][0:1, :], start=True, stop=False)
                        for h in range(HQ):
                            if qd == 0 and h % 16 == 0:
                                b = 2 * oc + h // 16
                                te.wait_ge(swo3[b % 4].h, 16 * (b // 4 + 1))
                            w_ap = wob[(2 * oc + h // 16) % 4][:,
                                       (h % 16) * OC6:(h % 16 + 1) * OC6]
                            for j in range(4):
                                tt = 4 * qd + j
                                ins = te.matmul(
                                    pq4[j][:],
                                    big[:, h * T + tt * 128: h * T + (tt + 1) * 128],
                                    w_ap, start=False, stop=(h == HQ - 1))
                                if h == HQ - 1:
                                    s.smm.inc(ins, 1)

            @blk.vector
            def _(v):
                for k in range(NK6):
                    v.wait_ge(s.smm.h, k + 1)
                    v.wait_ge(sgi[k % 2].h, 32 * (k // 2 + 1))
                    v.wait_ge(sxi[k % 2].h, 16 * (k // 2 + 1))
                    if k >= 2:
                        v.wait_ge(sos[k % 2].h, 16 * (k // 2))
                    v.tensor_tensor(ost[k % 2][:], psl[k % 8][:], gin[k % 2][:],
                                    op=ALU.mult)
                    s.oev.inc(v.tensor_tensor(ost[k % 2][:], ost[k % 2][:], xin[k % 2][:],
                                              op=ALU.add), 1)

    if DEBUG_HW:
        with nc.semaphore("s_dbgf") as sdbg, nc.Block() as blk:
            @blk.gpsimd
            def _(g):
                g.wait_ge(sos[0].h, 16 * (NK6 // 2))
                g.wait_ge(sos[1].h, 16 * (NK6 // 2))
                g.wait_ge(sdbgx.h, 16)
                g.dma_start(out=dbg_att, in_=att_s).then_inc(sdbg, 16)
                g.dma_start(out=dbg_g, in_=g_s).then_inc(sdbg, 16)
                g.dma_start(out=dbg_kvt, in_=kvt_sb[:]).then_inc(sdbg, 16)
                g.dma_start(out=dbg_nrm, in_=norms[:]).then_inc(sdbg, 16)
                g.dma_start(out=dbg_rcp, in_=recip[:]).then_inc(sdbg, 16)
                g.dma_start(out=dbg_big, in_=big[:]).then_inc(sdbg, 16)
                g.wait_ge(sdbg, 96)

    es.close()
    return nc


def kernel(**inputs):
    gi = lambda n: np.asarray(inputs[n])
    hs = gi("L0_attn_hidden_states").astype(np.float32, copy=False)
    Wq = gi("L0_attn_Wq").astype(np.float32, copy=False)
    bq = gi("L0_attn_bq").astype(np.float32, copy=False)
    Wk = gi("L0_attn_Wk").astype(np.float32, copy=False)
    bk = gi("L0_attn_bk").astype(np.float32, copy=False)
    Wv = gi("L0_attn_Wv").astype(np.float32, copy=False)
    bv = gi("L0_attn_bv").astype(np.float32, copy=False)
    Wo = gi("L0_attn_Wo").astype(np.float32, copy=False)
    bo = gi("L0_attn_bo").astype(np.float32, copy=False)
    pkw = gi("L0_ffn_packed_weight").astype(np.int32, copy=False)
    scl = gi("L0_ffn_scale").astype(np.float32, copy=False)
    fbi = gi("L0_ffn_bias").astype(np.float32, copy=False)

    TQ = np.ascontiguousarray(Wq.reshape(HQ, 128, NI, 128).transpose(0, 3, 2, 1))
    TK = np.ascontiguousarray(Wk.reshape(HKV, 128, NI, 128).transpose(0, 3, 2, 1))
    TV = np.ascontiguousarray(Wv.reshape(HKV, 128, NI, 128).transpose(0, 3, 2, 1))
    TO = np.ascontiguousarray(Wo.reshape(8, 512, 2, 16, 128).transpose(0, 2, 4, 3, 1))
    SC = np.ascontiguousarray(
        scl.reshape(NI, 128, 32).transpose(1, 0, 2).reshape(128, NI * 32))
    BQ = np.ascontiguousarray(bq.reshape(HQ, 128).T)
    BK = np.ascontiguousarray(bk.reshape(HKV, 128).T)
    BV = np.ascontiguousarray(bv.reshape(HKV, 128).T)
    BOR = np.ascontiguousarray(bo.reshape(1, D))
    FBR = np.ascontiguousarray(fbi.reshape(1, D))

    nc = build_program()
    in_maps = []
    for c in range(8):
        in_maps.append(dict(
            x=np.ascontiguousarray(hs[c]), tq=TQ, tk=TK, tv=TV, to=TO,
            pk=pkw, sc=SC, bq=BQ, bk=BK, bv=BV, bor=BOR, fbr=FBR,
        ))
    res = run_bass_kernel_spmd(nc, in_maps, core_ids=list(range(8)))
    return np.stack([r["out"] for r in res.results], axis=0)


if __name__ == "__main__":
    nc = build_program()
    print("instructions:", nc.n_instructions())
    print("sbuf remaining:", nc.sbuf_bytes_remaining)
